# version 02
# baseline (speedup 1.0000x reference)
import contextlib
import numpy as np
import concourse.bass as bass
import concourse.mybir as mybir
from concourse.bass_utils import run_bass_kernel_spmd

F32 = mybir.dt.float32
BF16 = mybir.dt.bfloat16
AF = mybir.ActivationFunctionType
ALU = mybir.AluOpType
AX = mybir.AxisListType

D = 4096
DC = 32
T = 1024
NH = 8
LSEC = 4
LT = LSEC * T
SCALE = float(1.0 / np.sqrt(128.0))
EXPC = 12.0
EPS = 1e-5
NEG = -1.0e30
ENGS = ("sync", "scalar", "vector", "tensor", "gpsimd")


class P:

    def __init__(self, nc, sems):
        self.nc = nc
        self.prog = {e: [] for e in ENGS}
        self.sems = sems
        self.c = {k: 0 for k in sems}
        self.wfp_free = [0, 0]
        self.wbf_free = [0, 0]

    def _mkwait(self, wait):
        w = tuple((self.sems[s], v) for (s, v) in (wait or ()) if v > 0)

        def do(q):
            for s, v in w:
                q.wait_ge(s, v)
        return do

    def dma(self, eng, out, in_, wait=None, sem="sLd"):
        self.c[sem] += 16
        wv, s = self._mkwait(wait), self.sems[sem]

        def f(q, out=out, in_=in_):
            wv(q)
            q.dma_start(out=out, in_=in_).then_inc(s, 16)
        self.prog[eng].append(f)
        return self.c[sem]

    def op(self, eng, fn, sem, wait=None):
        wv = self._mkwait(wait)
        if sem is not None:
            self.c[sem] += 1
            s = self.sems[sem]
        else:
            s = None

        def f(q, fn=fn):
            wv(q)
            ins = fn(q)
            if s is not None:
                ins.then_inc(s, 1)
        self.prog[eng].append(f)
        return self.c[sem] if sem else None

    def raw(self, eng, fn):
        self.prog[eng].append(fn)

    def wait(self, eng, sem, val):
        if val > 0:
            self.prog[eng].append(
                lambda q, s=self.sems[sem], v=val: q.wait_ge(s, v))


def _stage(inputs):
    f = np.ascontiguousarray
    x = np.asarray(inputs["L0_attn_x"], dtype=np.float32)

    def lhsT4(w, kc, fc):
        return f(np.asarray(w, np.float32)
                 .reshape(kc, 128, fc, 128).transpose(2, 1, 0, 3))

    def wT4(w):
        return f(np.asarray(w, np.float32)
                 .reshape(32, 128, 32, 128).transpose(0, 3, 2, 1))

    def vec(v, n=32):
        return f(np.asarray(v, np.float32).reshape(n, 128).T)

    wq, wk, wv = (wT4(inputs["L0_attn_W" + s]) for s in "qkv")
    shared = {
        "woT": wT4(inputs["L0_attn_Wo"]),
        "wup": lhsT4(inputs["L0_ffn_up_w"], 32, 64),
        "wdn": f(np.asarray(inputs["L0_ffn_down_w"], np.float32)
                 .reshape(64, 128, 8, 512).transpose(2, 1, 0, 3)),
        "wf1": lhsT4(inputs["L0_ffn_fc1_w"], 32, 16),
        "wf2": lhsT4(inputs["L0_ffn_fc2_w"], 16, 32),
        "wex": lhsT4(inputs["L0_ffn_expert_w"], 16, 16),
        "wgate": f(np.asarray(inputs["L0_ffn_gate_w"], np.float32)
                   .reshape(32, 128, 8).transpose(1, 0, 2)),
        "bo": vec(inputs["L0_attn_bo"]),
        "gam": vec(inputs["L0_ffn_ln_gamma"]),
        "bet": vec(inputs["L0_ffn_ln_beta"]),
        "band": f(np.where(np.arange(128)[:, None]
                           <= np.arange(896)[None, :] - 384,
                           np.float32(0.0), np.float32(NEG))),
        "ident": np.eye(128, dtype=np.float32),
        "ones": np.ones((128, 128), dtype=np.float32),
    }
    bq = np.asarray(inputs["L0_attn_bq"], np.float32)
    bk = np.asarray(inputs["L0_attn_bk"], np.float32)
    bv = np.asarray(inputs["L0_attn_bv"], np.float32)
    percore = []
    for c in range(8):
        b, jg = c // 4, c % 4
        hs = slice(jg * NH * 128, (jg + 1) * NH * 128)
        xt = f(x[b].T)
        percore.append({
            "xTfull": f(xt.reshape(DC, 128, LSEC, T).transpose(2, 1, 0, 3)),
            "xTown": f(xt[:, jg * T:(jg + 1) * T]
                       .reshape(DC, 128, T).transpose(1, 0, 2)),
            "wqT": f(wq[jg * NH:(jg + 1) * NH]),
            "wkT": f(wk[jg * NH:(jg + 1) * NH]),
            "wvT": f(wv[jg * NH:(jg + 1) * NH]),
            "bqs": vec(bq[hs], NH), "bks": vec(bk[hs], NH),
            "bvs": vec(bv[hs], NH),
        })
    return shared, percore


def build_program(taps=(), stop_after=None):
    nc = bass.Bass("TRN2", target_bir_lowering=False, debug=False)
    dt = nc.dram_tensor

    xTfull = dt("xTfull", [LSEC, 128, DC, T], F32, kind="ExternalInput").ap()
    xTown = dt("xTown", [128, DC, T], F32, kind="ExternalInput").ap()
    wIn = {}
    for nm, sh in (("wqT", [NH, 128, DC, 128]), ("wkT", [NH, 128, DC, 128]),
                   ("wvT", [NH, 128, DC, 128]), ("woT", [32, 128, DC, 128]),
                   ("wup", [64, 128, DC, 128]), ("wdn", [8, 128, 64, 512]),
                   ("wf1", [16, 128, DC, 128]), ("wf2", [32, 128, 16, 128]),
                   ("wex", [16, 128, 16, 128]), ("wgate", [128, DC, 8])):
        wIn[nm] = dt(nm, sh, F32, kind="ExternalInput").ap()
    vIn = {}
    for nm, n in (("bqs", NH), ("bks", NH), ("bvs", NH), ("bo", 32),
                  ("gam", 32), ("bet", 32)):
        vIn[nm] = dt(nm, [128, n], F32, kind="ExternalInput").ap()
    band_i = dt("band", [128, 896], F32, kind="ExternalInput").ap()
    ident_i = dt("ident", [128, 128], F32, kind="ExternalInput").ap()
    ones_i = dt("ones", [128, 128], F32, kind="ExternalInput").ap()
    outT = dt("outT", [32, 128, T], F32, kind="ExternalOutput").ap()

    kvin = dt("kvin", [2048, LT], BF16, kind="Internal")
    attpub = dt("attpub", [NH * 128, LT], BF16, kind="Internal")
    attg = dt("attg", [4 * NH * 128, LT], BF16, kind="Internal")
    attown = dt("attown", [4096, T], BF16, kind="Internal")
    h1d = dt("h1d", [64, 128, T], BF16, kind="Internal")
    rtd = dt("rtd", [32, 128, T], BF16, kind="Internal")

    tap_t = {}
    for name, shape, dtp in (
            ("q", [NH * 128, LT], BF16), ("kv", [2048, LT], BF16),
            ("att", [NH * 128, LT], BF16), ("attow", [4096, T], BF16),
            ("r1", [4096, T], BF16), ("xlat", [2048, T], BF16),
            ("moe", [2048, T], BF16), ("rt", [4096, T], BF16),
            ("h1", [8192, T], BF16), ("sb", [128, T], F32),
            ("lg", [128, 64], F32), ("st", [1, T], F32),
            ("t0", [4096, 512], BF16), ("r2", [4096, T], BF16)):
        if name in taps:
            tap_t[name] = dt("tap_" + name, shape, dtp,
                             kind="ExternalOutput").ap()

    al = nc.alloc_sbuf_tensor
    stream = al("stream", [128, DC * T], BF16)
    qo = al("qo", [128, DC * T], BF16)
    U = al("U", [128, 73728], mybir.dt.uint8)
    ub = nc.lookup_mloc(U).addr

    def at(name, shape, dtp, off):
        return nc.alloc_sbuf_tensor_at(name, shape, dtp, offset=ub + off)

    wfp = [at("wfp0", [128, 2048], F32, 0), at("wfp1", [128, 2048], F32, 8192)]
    wbf = [at("wbf0", [128, 4096], BF16, 16384),
           at("wbf1", [128, 4096], BF16, 24576)]
    xstage = [at("xstage0", [128, 2048], F32, 32768),
              at("xstage1", [128, 2048], F32, 40960)]
    kvstage = [at("kvstage0", [128, 1024], BF16, 49152),
               at("kvstage1", [128, 1024], BF16, 51200)]
    kt = [at("kt0", [128, LT], BF16, 0), at("kt1", [128, LT], BF16, 8192)]
    vh = [at("vh0", [128, LT], BF16, 16384),
          at("vh1", [128, LT], BF16, 24576)]
    vt = at("vt", [128, LT], BF16, 32768)
    pt = [at("pt0", [128, 512], BF16, 40960),
          at("pt1", [128, 512], BF16, 41984)]
    linv = at("linv", [128, 512], F32, 43008)
    sq4 = at("sq4", [128, 4 * 512], BF16, 49152)
    tmp1 = at("tmp1", [128, 512], BF16, 53248)
    tmp2 = at("tmp2", [128, 512], BF16, 54272)
    m_sb = at("m_sb", [128, 512], F32, 55296)
    r_sb = at("r_sb", [128, 512], F32, 57344)
    v_sb = at("v_sb", [128, 512], F32, 59392)
    ln_t = at("ln_t", [128, 512], F32, 61440)
    lg = at("lg", [128, 64], F32, 63488)
    t2k = at("t2k", [128, 32], F32, 63808)
    gate_f = at("gate_f", [128, 256], F32, 63936)
    gate_b = at("gate_b", [128, 256], BF16, 64960)
    sT = at("sT", [1, T], F32, 65472)
    s_b = at("s_b", [128, T], F32, 69568)
    rst = at("rst", [128, 4 * 512], BF16, 53248)
    h1st = at("h1st", [128, 4 * T], BF16, 32768)
    rel32 = at("rel32", [128, 512], F32, 40960)
    t0 = at("t0", [128, 32 * 512], BF16, 0)
    dfp = [at("dfp0", [128, 1024], F32, 32768),
           at("dfp1", [128, 1024], F32, 36864)]
    dbf = [at("dbf0", [128, 1024], BF16, 40960),
           at("dbf1", [128, 1024], BF16, 43008)]
    s1_sb = at("s1_sb", [128, 512], F32, 65472)
    s2_sb = at("s2_sb", [128, 512], F32, 67520)
    ost = at("ost", [128, 2 * 512], F32, 36864)

    band = al("band_s", [128, 896], F32)
    ident_f = al("ident_f", [128, 128], F32)
    ones_f = al("ones_f", [128, 128], F32)
    ident_b = al("ident_b", [128, 128], BF16)
    ones_b = al("ones_b", [128, 128], BF16)
    vS = {nm: al(nm + "_s", [128, n], F32)
          for nm, n in (("bqs", NH), ("bks", NH), ("bvs", NH), ("bo", 32),
                        ("gam", 32), ("bet", 32))}
    cneg = al("cneg", [128, 1], F32)
    ceps = al("ceps", [128, 1], F32)

    ctx = contextlib.ExitStack()
    sems = {n: ctx.enter_context(nc.semaphore(n))
            for n in ("sLd", "sSt", "pe", "ve", "ac", "po", "cc")}
    p = P(nc, sems)

    ps = [ctx.enter_context(nc.psum_tensor(f"ps{i}", [128, 512], F32))
          for i in range(7)]
    psbt_g = nc.psum_tensor("psbt", [128, 1024], BF16)
    psbt = psbt_g.__enter__()

    for dst, src in ((band, band_i), (ident_f, ident_i), (ones_f, ones_i),
                     (vS["bqs"], vIn["bqs"]), (vS["bks"], vIn["bks"]),
                     (vS["bvs"], vIn["bvs"]), (vS["bo"], vIn["bo"]),
                     (vS["gam"], vIn["gam"]), (vS["bet"], vIn["bet"])):
        p.dma("sync", dst[:, :], src)
    cl = p.c["sLd"]
    p.op("vector", lambda q: q.tensor_copy(ident_b[:, :], ident_f[:, :]),
         "ve", wait=[("sLd", cl)])
    p.op("vector", lambda q: q.tensor_copy(ones_b[:, :], ones_f[:, :]), "ve")
    p.op("vector", lambda q: q.memset(cneg[:, :], -EXPC), "ve")
    p.op("vector", lambda q: q.memset(ceps[:, :], EPS), "ve")
    consts_ve = p.c["ve"]

    def stream_tile(piece_aps, buf):
        off, ev = 0, 0
        for i, ap_ in enumerate(piece_aps):
            fb = i % 2
            ld = p.dma("sync", wfp[fb][:, :], ap_,
                       wait=[("po", p.wfp_free[fb])])
            cw = [("sLd", ld)]
            if i == 0:
                cw.append(("pe", p.wbf_free[buf]))
            ev = p.op("gpsimd",
                      lambda q, s=wfp[fb][:, :],
                      d=wbf[buf][:, off:off + 2048]: q.tensor_copy(d, s),
                      "po", wait=cw)
            p.wfp_free[fb] = ev
            off += 2048
        return ev

    def halves(tile_ap):
        return [tile_ap[:, 0:16, :], tile_ap[:, 16:32, :]]

    def mmgroup(buf, psA, psB, src, nk, mmwait):
        def g(q, buf=buf, nk=nk):
            ins = None
            for k in range(nk):
                lh = wbf[buf][:, k * 128:k * 128 + 128]
                q.matmul(psA[:, :], lh, src[:, k * T:k * T + 512],
                         start=(k == 0), stop=(k == nk - 1))
                ins = q.matmul(psB[:, :], lh, src[:, k * T + 512:k * T + T],
                               start=(k == 0), stop=(k == nk - 1))
            return ins
        ev = p.op("tensor", g, "pe", wait=mmwait)
        p.wbf_free[buf] = ev
        return ev

    kvst_free = [0, 0]
    ev_last_set = [0, 0]
    A_pe_last = 0
    xst_free = [0, 0]
    for sec in range(LSEC):
        pe_guard = A_pe_last
        for i in range(16):
            xb = i % 2
            ld = p.dma("sync", xstage[xb][:, :],
                       xTfull[sec, :, 2 * i:2 * i + 2, :],
                       wait=[("ve", xst_free[xb])])
            cw = [("sLd", ld)]
            if i < 2 and pe_guard:
                cw.append(("pe", pe_guard))
            xst_free[xb] = p.op(
                "vector",
                lambda q, s=xstage[xb][:, :],
                d=stream[:, 2 * i * T:(2 * i + 2) * T]: q.tensor_copy(d, s),
                "ve", wait=cw)
        xsec_ve = p.c["ve"]

        for wsrc, bias_t, kind in ((wIn["wqT"], vS["bqs"], "q"),
                                   (wIn["wkT"], vS["bks"], "k"),
                                   (wIn["wvT"], vS["bvs"], "v")):
            for ft in range(NH):
                buf = ft % 2
                po_ev = stream_tile(halves(wsrc[ft]), buf)
                psA, psB = ps[2 * buf], ps[2 * buf + 1]
                mm_ev = mmgroup(buf, psA, psB, stream, DC,
                                [("po", po_ev), ("ve", xsec_ve),
                                 ("ve", ev_last_set[buf])])
                A_pe_last = mm_ev
                if kind == "q":
                    for th in range(2):
                        dst = qo[:, ft * LT + sec * T + th * 512:
                                 ft * LT + sec * T + th * 512 + 512]
                        p.op("vector",
                             lambda q, d=dst, s=(psA if th == 0 else psB),
                             bt=bias_t[:, ft:ft + 1]:
                             q.tensor_scalar(d, s[:, :], bt, None, ALU.add),
                             "ve", wait=[("pe", mm_ev)])
                    ev_last_set[buf] = p.c["ve"]
                else:
                    kb = ft % 2
                    row0 = (0 if kind == "k" else 1024) + ft * 128
                    for th in range(2):
                        w = [("pe", mm_ev)]
                        if th == 0:
                            w.append(("sSt", kvst_free[kb]))
                        p.op("vector",
                             lambda q,
                             d=kvstage[kb][:, th * 512:th * 512 + 512],
                             s=(psA if th == 0 else psB),
                             bt=bias_t[:, ft:ft + 1]:
                             q.tensor_scalar(d, s[:, :], bt, None, ALU.add),
                             "ve", wait=w)
                    ev_last_set[buf] = p.c["ve"]
                    kvst_free[kb] = p.dma(
                        "scalar", kvin[row0:row0 + 128, sec * T:(sec + 1) * T],
                        kvstage[kb][:, :], wait=[("ve", p.c["ve"])],
                        sem="sSt")
    A_kv_st = p.c["sSt"]
    A_q_ac = p.c["ve"]
    if "q" in tap_t:
        for ftt in range(NH):
            p.dma("scalar", tap_t["q"][ftt * 128:(ftt + 1) * 128, :],
                  qo[:, ftt * LT:(ftt + 1) * LT], wait=[("ve", A_q_ac)],
                  sem="sSt")
    if "kv" in tap_t:
        p.dma("scalar",
              tap_t["kv"].rearrange("(a p) t -> p a t", p=128),
              kvin[:, :].rearrange("(a p) t -> p a t", p=128),
              wait=[("sSt", A_kv_st)], sem="sSt")

    B_pub = 0
    if stop_after not in ("A",):
        psS, psO, psL = ps[0:2], ps[2:4], ps[4:6]
        kt_free = [0, 0]
        vh_free = [0, 0]
        vt_free = 0
        last_exp = [0, 0]
        last_ov = [0, 0]
        last_oevac = [0, 0]
        psbt_free = 0
        for h in range(NH):
            hb = h % 2
            kld = p.dma("sync", kt[hb][:, :], kvin[h * 128:h * 128 + 128, :],
                        wait=[("sSt", A_kv_st), ("pe", kt_free[hb])])
            vld = p.dma("sync", vt[:, :],
                        kvin[1024 + h * 128:1024 + h * 128 + 128, :],
                        wait=[("sSt", A_kv_st), ("pe", vt_free)])
            for bch in range(4):
                w = [("sLd", vld), ("ve", psbt_free)]
                if bch == 0:
                    w.append(("pe", vh_free[hb]))

                def trb(q, bch=bch):
                    ins = None
                    for i in range(8):
                        kc = bch * 8 + i
                        ins = q.transpose(psbt[:, i * 128:(i + 1) * 128],
                                          vt[:, kc * 128:(kc + 1) * 128],
                                          ident_b[:, :])
                    return ins
                te = p.op("tensor", trb, "pe", wait=w)
                psbt_free = p.op(
                    "vector",
                    lambda q, d=vh[hb][:, bch * 1024:(bch + 1) * 1024]:
                    q.tensor_copy(d, psbt[:, :]), "ve", wait=[("pe", te)])
            vt_free = p.c["pe"]
            vh_ready = p.c["ve"]

            for qh in range(8):
                nkc = 4 * qh + 4
                ob = qh % 2
                for kc in range(nkc):
                    sb_ = kc % 2
                    w = [("sLd", kld), ("ac", last_exp[sb_])]
                    if kc == 0:
                        w += [("ve", last_oevac[ob]), ("ve", A_q_ac)]

                    def smm(q, sb_=sb_, kc=kc, qh=qh, hb=hb, h=h):
                        return q.matmul(
                            psS[sb_][:, :],
                            kt[hb][:, kc * 128:(kc + 1) * 128],
                            qo[:, h * LT + qh * 512:h * LT + qh * 512 + 512],
                            start=True, stop=True)
                    smm_ev = p.op("tensor", smm, "pe", wait=w)
                    acw = [("pe", smm_ev), ("pe", last_ov[sb_])]
                    if kc >= 4 * qh:
                        dlt = 384 + qh * 512 - kc * 128
                        mv = p.op("vector",
                                  lambda q, sb_=sb_, dlt=dlt:
                                  q.tensor_add(psS[sb_][:, :], psS[sb_][:, :],
                                               band[:, dlt:dlt + 512]),
                                  "ve", wait=[("pe", smm_ev)])
                        acw = [("ve", mv), ("pe", last_ov[sb_])]
                    last_exp[sb_] = p.op(
                        "scalar",
                        lambda q, sb_=sb_:
                        q.activation(pt[sb_][:, :], psS[sb_][:, :], AF.Exp,
                                     bias=cneg[:, 0:1], scale=SCALE),
                        "ac", wait=acw)

                    def ovmm(q, hb=hb, sb_=sb_, ob=ob, kc=kc, nkc=nkc):
                        q.matmul(psO[ob][:, :],
                                 vh[hb][:, kc * 128:(kc + 1) * 128],
                                 pt[sb_][:, :],
                                 start=(kc == 0), stop=(kc == nkc - 1))
                        return q.matmul(psL[ob][:, :], ones_b[:, :],
                                        pt[sb_][:, :],
                                        start=(kc == 0), stop=(kc == nkc - 1))
                    last_ov[sb_] = p.op(
                        "tensor", ovmm, "pe",
                        wait=[("ac", last_exp[sb_]), ("ve", vh_ready)])
                ov_ev = p.c["pe"]
                rcv = p.op("vector",
                           lambda q, ob=ob: q.reciprocal(linv[:, :],
                                                         psL[ob][:, :]),
                           "ve", wait=[("pe", ov_ev)])
                last_oevac[ob] = p.op(
                    "vector",
                    lambda q, ob=ob,
                    d=qo[:, h * LT + qh * 512:h * LT + qh * 512 + 512]:
                    q.scalar_tensor_tensor(d, psO[ob][:, :], 1.0, linv[:, :],
                                           ALU.mult, ALU.mult),
                    "ve", wait=[("ve", rcv)])
            kt_free[hb] = p.c["pe"]
            vh_free[hb] = p.c["pe"]
            p.dma("scalar", attpub[h * 128:(h + 1) * 128, :],
                  qo[:, h * LT:(h + 1) * LT], wait=[("ve", p.c["ve"])],
                  sem="sSt")
        B_pub = p.c["sSt"]
        if "att" in tap_t:
            p.dma("scalar",
                  tap_t["att"].rearrange("(a p) t -> p a t", p=128),
                  attpub[:, :].rearrange("(a p) t -> p a t", p=128),
                  wait=[("sSt", B_pub)], sem="sSt")

    attq_ld = xown_ve = 0
    if stop_after not in ("A", "B"):
        p.c["cc"] += 1

        def coll(q, B_pub=B_pub):
            q.wait_ge(p.sems["sSt"], B_pub)
            q.collective_compute(
                "AllGather", ALU.bypass,
                replica_groups=[[0, 1, 2, 3], [4, 5, 6, 7]],
                ins=[attpub.ap().opt()], outs=[attg.ap().opt()],
            ).then_inc(p.sems["cc"], 1)
        p.raw("gpsimd", coll)

        p.c["sLd"] += 64
        ext_ev = p.c["sLd"]

        def extract(q):
            q.wait_ge(p.sems["cc"], 1)
            for arm in nc.switch_core_id(engines=[q], n=128):
                jg = (arm.logical % 8) % 4
                for r4 in range(4):
                    q.dma_start(
                        out=attown[r4 * 1024:(r4 + 1) * 1024, :],
                        in_=attg[r4 * 1024:(r4 + 1) * 1024,
                                 jg * T:(jg + 1) * T]
                    ).then_inc(p.sems["sLd"], 16)
        p.raw("sync", extract)
        p.wait("sync", "pe", p.c["pe"])
        if "attow" in tap_t:
            p.dma("scalar",
                  tap_t["attow"].rearrange("(a p) t -> p a t", p=128),
                  attown[:, :].rearrange("(a p) t -> p a t", p=128),
                  wait=[("sLd", ext_ev)], sem="sSt")

        for cc_ in range(32):
            p.dma("sync", qo[:, cc_ * T:(cc_ + 1) * T],
                  attown[cc_ * 128:(cc_ + 1) * 128, :],
                  wait=[("sLd", ext_ev)] if cc_ == 0 else None)
        attq_ld = p.c["sLd"]

        xst2 = [p.c["ve"], p.c["ve"]]
        for i in range(16):
            xb = i % 2
            ld = p.dma("sync", xstage[xb][:, :], xTown[:, 2 * i:2 * i + 2, :],
                       wait=[("ve", xst2[xb])])
            xst2[xb] = p.op(
                "vector",
                lambda q, s=xstage[xb][:, :],
                d=stream[:, 2 * i * T:(2 * i + 2) * T]: q.tensor_copy(d, s),
                "ve", wait=[("sLd", ld)])
        xown_ve = p.c["ve"]

    C_pe_last = 0
    r1_ve = 0
    if stop_after not in ("A", "B"):
        psA, psB = ps[0], ps[1]
        psS1, psS2 = {0: ps[2], 1: ps[3]}, {0: ps[4], 1: ps[5]}
        prev = None
        ev_last = 0

        def statmm(pft, pev, paev):
            def g(q, pft=pft):
                ins = None
                for th in range(2):
                    sl = stream[:, pft * T + th * 512:pft * T + th * 512 + 512]
                    q.matmul(psS1[th][:, :], ones_b[:, :], sl,
                             start=(pft == 0), stop=(pft == 31))
                    ins = q.matmul(psS2[th][:, :], ones_b[:, :],
                                   sq4[:, ((pft % 2) * 2 + th) * 512:
                                       ((pft % 2) * 2 + th) * 512 + 512],
                                   start=(pft == 0), stop=(pft == 31))
                return ins
            return p.op("tensor", g, "pe",
                        wait=[("ve", pev), ("ac", paev)])

        for ft in range(32):
            buf = ft % 2
            po_ev = stream_tile(halves(wIn["woT"][ft]), buf)
            w = [("po", po_ev), ("ve", ev_last), ("sLd", attq_ld),
                 ("ve", xown_ve)]
            mm_ev = mmgroup(buf, psA, psB, qo, DC, w)
            if prev:
                statmm(*prev)
            for th in range(2):
                sl = stream[:, ft * T + th * 512:ft * T + th * 512 + 512]
                p.op("vector",
                     lambda q, sl=sl, s=(psA if th == 0 else psB),
                     bt=vS["bo"][:, ft:ft + 1]:
                     q.scalar_tensor_tensor(sl, s[:, :], bt, sl,
                                            ALU.add, ALU.add),
                     "ve", wait=[("pe", mm_ev)])
                p.op("scalar",
                     lambda q, sl=sl,
                     d=sq4[:, ((ft % 2) * 2 + th) * 512:
                           ((ft % 2) * 2 + th) * 512 + 512]:
                     q.activation(d, sl, AF.Square),
                     "ac", wait=[("ve", p.c["ve"]), ("pe", p.c["pe"])])
            ev_last = p.c["ve"]
            prev = (ft, p.c["ve"], p.c["ac"])
        statmm(*prev)
        stat_pe = p.c["pe"]
        C_pe_last = stat_pe

        for th in range(2):
            fin = _ln_finalize(p, psS1[th], psS2[th], stat_pe, m_sb, r_sb,
                               v_sb, ln_t, 1.0 / 4096.0, ceps)
            for ft in range(32):
                sl = stream[:, ft * T + th * 512:ft * T + th * 512 + 512]
                p.op("vector",
                     lambda q, sl=sl:
                     q.tensor_sub(tmp1[:, :], sl, m_sb[:, :]),
                     "ve", wait=[("ac", fin)] if ft == 0 else None)
                p.op("vector",
                     lambda q, sl=sl:
                     q.tensor_mul(sl, tmp1[:, :], r_sb[:, :]), "ve")
        r1_ve = p.c["ve"]
        if "r1" in tap_t:
            p.dma("scalar",
                  tap_t["r1"].rearrange("(a p) t -> p a t", p=128),
                  stream[:, :].rearrange("p (a t) -> p a t", t=T),
                  wait=[("ve", r1_ve)], sem="sSt")

    if stop_after not in ("A", "B", "C"):
        _phase_d(p, nc, ctx, locals())

    p.wait("sync", "sSt", p.c["sSt"])
    p.wait("sync", "sLd", p.c["sLd"])
    p.wait("scalar", "sSt", p.c["sSt"])

    with nc.Block() as blk:
        blk.sync(lambda q: [f(q) for f in p.prog["sync"]])
        blk.scalar(lambda q: [f(q) for f in p.prog["scalar"]])
        blk.vector(lambda q: [f(q) for f in p.prog["vector"]])
        blk.tensor(lambda q: [f(q) for f in p.prog["tensor"]])
        blk.gpsimd(lambda q: [f(q) for f in p.prog["gpsimd"]])
    return nc


def _ln_finalize(p, psS1, psS2, stat_pe, m_sb, r_sb, v_sb, ln_t, inv_n, ceps):
    p.op("vector",
         lambda q: q.tensor_scalar(m_sb[:, :], psS1[:, :], inv_n, None,
                                   ALU.mult),
         "ve", wait=[("pe", stat_pe)])
    e1 = p.op("vector",
              lambda q: q.tensor_scalar(v_sb[:, :], psS2[:, :], inv_n, None,
                                        ALU.mult), "ve")
    e2 = p.op("scalar",
              lambda q: q.activation(ln_t[:, :], m_sb[:, :], AF.Square),
              "ac", wait=[("ve", e1)])
    e3 = p.op("vector",
              lambda q: q.tensor_sub(v_sb[:, :], v_sb[:, :], ln_t[:, :]),
              "ve", wait=[("ac", e2)])
    p.op("scalar",
         lambda q, ceps=ceps: q.activation(ln_t[:, :], v_sb[:, :], AF.Ln,
                                           bias=ceps[:, 0:1]),
         "ac", wait=[("ve", e3)])
    return p.op("scalar",
                lambda q: q.activation(r_sb[:, :], ln_t[:, :], AF.Exp,
                                       scale=-0.5), "ac")


def _phase_d(p, nc, ctx, L):
    g = L
    ps, psbt_g = g["ps"], g["psbt_g"]
    stream, qo = g["stream"], g["qo"]
    wIn, vS = g["wIn"], g["vS"]
    wfp, wbf = g["wfp"], g["wbf"]
    sq4, tmp1, tmp2 = g["sq4"], g["tmp1"], g["tmp2"]
    m_sb, r_sb, v_sb, ln_t = g["m_sb"], g["r_sb"], g["v_sb"], g["ln_t"]
    s1_sb, s2_sb = g["s1_sb"], g["s2_sb"]
    t0, h1st, rst, ost, rel32 = g["t0"], g["h1st"], g["rst"], g["ost"], g["rel32"]
    dfp, dbf = g["dfp"], g["dbf"]
    gate_f, gate_b = g["gate_f"], g["gate_b"]
    sT, s_b, lg, t2k = g["sT"], g["s_b"], g["lg"], g["t2k"]
    ones_f, ones_b, ident_f = g["ones_f"], g["ones_b"], g["ident_f"]
    h1d, rtd, outT = g["h1d"], g["rtd"], g["outT"]
    tap_t = g["tap_t"]
    r1_ve, C_pe_last = g["r1_ve"], g["C_pe_last"]
    ceps = g["ceps"]
    stream_tile, halves, mmgroup = g["stream_tile"], g["halves"], g["mmgroup"]
    T_, DC_ = T, DC

    psbt_g.__exit__(None, None, None)
    ps8 = ctx.enter_context(nc.psum_tensor("ps8", [128, 512], F32))

    gld = p.dma("sync", gate_f[:, :], wIn["wgate"])
    gcast = p.op("gpsimd",
                 lambda q: q.tensor_copy(gate_b[:, :], gate_f[:, :]),
                 "po", wait=[("sLd", gld)])
    ev_ac = 0
    for tt in range(8):
        def gmm(q, tt=tt):
            ins = None
            for dc in range(DC_):
                ins = q.matmul(ps[6][:, 0:8],
                               stream[:, dc * T_ + tt * 128:
                                      dc * T_ + tt * 128 + 128],
                               gate_b[:, dc * 8:dc * 8 + 8],
                               start=(dc == 0), stop=(dc == DC_ - 1))
            return ins
        w = [("po", gcast), ("ve", r1_ve)]
        if ev_ac:
            w.append(("ac", ev_ac))
        gev = p.op("tensor", gmm, "pe", wait=w)
        ev_ac = p.op("scalar",
                     lambda q, tt=tt:
                     q.copy(lg[:, tt * 8:tt * 8 + 8], ps[6][:, 0:8]),
                     "ac", wait=[("pe", gev)])
    top2_ev = 0
    for tt in range(8):
        sl = lg[:, tt * 8:tt * 8 + 8]
        mx = p.op("vector", lambda q, sl=sl:
                  q.max(out=t2k[:, 8:16], in_=sl),
                  "ve", wait=[("ac", ev_ac)] if tt == 0 else None)
        top2_ev = p.op("vector", lambda q, tt=tt:
                       q.tensor_add(t2k[:, 24 + tt:25 + tt], t2k[:, 8:9],
                                    t2k[:, 9:10]), "ve",
                       wait=[("ve", mx)])
    sc_ve = 0
    for half in range(2):
        def strn(q, half=half):
            ins = None
            for i in range(4):
                tt = half * 4 + i
                ins = q.transpose(ps[5][0:1, i * 128:(i + 1) * 128],
                                  t2k[:, 24 + tt:25 + tt], ident_f[:, :])
            return ins
        w = [("ve", top2_ev)]
        if sc_ve:
            w.append(("ve", sc_ve))
        tev = p.op("tensor", strn, "pe", wait=w)
        sc_ve = p.op("vector",
                     lambda q, half=half:
                     q.tensor_copy(sT[0:1, half * 512:half * 512 + 512],
                                   ps[5][0:1, 0:512]),
                     "ve", wait=[("pe", tev)])
    sb_ve = 0
    for half in range(2):
        bev = p.op("tensor",
                   lambda q, half=half:
                   q.matmul(ps[5][:, :], ones_f[0:1, :],
                            sT[0:1, half * 512:half * 512 + 512],
                            start=True, stop=True),
                   "pe", wait=[("ve", sc_ve if half == 0 else sb_ve)])
        sb_ve = p.op("vector",
                     lambda q, half=half:
                     q.tensor_copy(s_b[:, half * 512:half * 512 + 512],
                                   ps[5][:, 0:512]),
                     "ve", wait=[("pe", bev)])

    snap = [("ve", p.c["ve"]), ("ac", p.c["ac"])]
    ev_set = [0, 0]
    for ft in range(16):
        buf = ft % 2
        po_ev = stream_tile(halves(wIn["wf1"][ft]), buf)
        mm_ev = mmgroup(buf, ps[2 * buf], ps[2 * buf + 1], stream, DC_,
                        [("po", po_ev), ("ve", r1_ve),
                         ("ac", ev_set[buf])] + (snap if ft < 2 else []))
        for th in range(2):
            w = [("pe", mm_ev)]
            if ft == 0 and th == 0:
                w.append(("pe", C_pe_last))
            p.op("scalar",
                 lambda q, d=qo[:, ft * T_ + th * 512:ft * T_ + th * 512 + 512],
                 s=(ps[2 * buf] if th == 0 else ps[2 * buf + 1]):
                 q.activation(d, s[:, :], AF.Copy), "ac", wait=w)
        ev_set[buf] = p.c["ac"]
    fc1_ac = p.c["ac"]
    if "xlat" in tap_t:
        p.dma("scalar", tap_t["xlat"].rearrange("(a p) t -> p a t", p=128),
              qo[:, 0:16 * T_].rearrange("p (a t) -> p a t", t=T_),
              wait=[("ac", fc1_ac)], sem="sSt")

    ev_set = [0, 0]
    for ft in range(16):
        buf = ft % 2
        po_ev = stream_tile([wIn["wex"][ft]], buf)
        mm_ev = mmgroup_n(p, wbf, buf, ps[2 * buf], ps[2 * buf + 1], qo, 16,
                          [("po", po_ev), ("ac", fc1_ac),
                           ("ve", ev_set[buf])])
        for th in range(2):
            ev_set[buf] = p.op(
                "vector",
                lambda q,
                d=qo[:, (16 + ft) * T_ + th * 512:(16 + ft) * T_ + th * 512 + 512],
                s=(ps[2 * buf] if th == 0 else ps[2 * buf + 1]),
                sb=s_b[:, th * 512:th * 512 + 512]:
                q.scalar_tensor_tensor(d, s[:, :], 1.0, sb, ALU.mult,
                                       ALU.mult),
                "ve", wait=[("pe", mm_ev), ("ve", sb_ve)])
    moe_ve = p.c["ve"]
    if "moe" in tap_t:
        p.dma("scalar", tap_t["moe"].rearrange("(a p) t -> p a t", p=128),
              qo[:, 16 * T_:32 * T_].rearrange("p (a t) -> p a t", t=T_),
              wait=[("ve", moe_ve)], sem="sSt")

    ev_set = [0, 0]
    rst_free = 0
    for ft in range(32):
        buf = ft % 2
        po_ev = stream_tile([wIn["wf2"][ft]], buf)
        mm_ev = mmgroup_n(p, wbf, buf, ps[2 * buf], ps[2 * buf + 1],
                          qo[:, 16 * T_:32 * T_], 16,
                          [("po", po_ev), ("ve", moe_ve),
                           ("ac", ev_set[buf])])
        ro = (ft % 2) * 1024
        for th in range(2):
            w = [("pe", mm_ev)]
            if th == 0:
                w.append(("sSt", rst_free))
            p.op("scalar",
                 lambda q, d=rst[:, ro + th * 512:ro + th * 512 + 512],
                 s=(ps[2 * buf] if th == 0 else ps[2 * buf + 1]):
                 q.activation(d, s[:, :], AF.Copy), "ac", wait=w)
        ev_set[buf] = p.c["ac"]
        rst_free = p.dma("scalar", rtd[ft], rst[:, ro:ro + 1024],
                         wait=[("ac", p.c["ac"])], sem="sSt")
    rt_st = p.c["sSt"]
    D4_pe = p.c["pe"]
    if "rt" in tap_t:
        p.dma("scalar", tap_t["rt"].rearrange("(a p) t -> p a t", p=128),
              rtd[:, :, :].rearrange("a p t -> p a t"),
              wait=[("sSt", rt_st)], sem="sSt")

    snap5 = [("ve", p.c["ve"]), ("ac", p.c["ac"])]
    ev_set = [0, 0]
    h1_free = 0
    for ft in range(64):
        buf = ft % 2
        po_ev = stream_tile(halves(wIn["wup"][ft]), buf)
        mm_ev = mmgroup(buf, ps[2 * buf], ps[2 * buf + 1], stream, DC_,
                        [("po", po_ev), ("ve", r1_ve), ("ac", ev_set[buf])]
                        + (snap5 if ft < 2 else []))
        so = (ft % 4) * 1024
        for th in range(2):
            w = [("pe", mm_ev)]
            if ft % 4 == 0 and th == 0:
                w.append(("sSt", h1_free))
            p.op("scalar",
                 lambda q, s=(ps[2 * buf] if th == 0 else ps[2 * buf + 1]):
                 q.activation(rel32[:, :], s[:, :], AF.Relu), "ac", wait=w)
            p.op("scalar",
                 lambda q, d=h1st[:, so + th * 512:so + th * 512 + 512]:
                 q.activation(d, rel32[:, :], AF.Square), "ac")
        ev_set[buf] = p.c["ac"]
        if ft % 4 == 3:
            h1_free = p.dma(
                "scalar", h1d[ft - 3:ft + 1].transpose([1, 0, 2]),
                h1st[:, :], wait=[("ac", p.c["ac"])], sem="sSt")
    h1_st = p.c["sSt"]
    if "h1" in tap_t:
        p.dma("scalar", tap_t["h1"].rearrange("(a p) t -> p a t", p=128),
              h1d[:, :, :].rearrange("a p t -> p a t"),
              wait=[("sSt", h1_st)], sem="sSt")

    setbank = [[ps[0], ps[1], ps[2], ps[3]], [ps[4], ps[5], ps[6], ps8]]
    out_st_total = 0
    prev_sev = 0
    dfp_free = [0, 0]
    dbf_free = [0, 0]
    p.wait("sync", "sSt", h1_st)
    p.wait("sync", "po", p.c["po"])
    for th in range(2):
        h1w = [("sSt", h1_st)]
        h1w.append(("pe", D4_pe if th == 0 else p.c["pe"]))
        if th == 0:
            h1w.append(("ve", p.c["ve"]))
        h1ld = 0
        for seg in range(8):
            h1ld = p.dma(
                "sync",
                qo[:, seg * 8 * 512:(seg + 1) * 8 * 512]
                .rearrange("p (a t) -> p a t", t=512),
                h1d[seg * 8:(seg + 1) * 8, :,
                    th * 512:th * 512 + 512].transpose([1, 0, 2]),
                wait=h1w if seg == 0 else None)
        set_free = [p.c["ve"], p.c["ve"]]
        rst_ld = p.c["ve"]
        stat_done = 0
        for fg in range(8):
            s_ = fg % 2
            banks = setbank[s_]
            rld = p.dma("sync",
                        rst[:, :].rearrange("p (a t) -> p a t", t=512),
                        rtd[fg * 4:fg * 4 + 4, :,
                            th * 512:th * 512 + 512].transpose([1, 0, 2]),
                        wait=[("sSt", rt_st), ("ve", rst_ld)])
            mmw_first = [("sLd", h1ld), ("ve", set_free[s_])]
            for k in range(32):
                buf = k % 2
                ld = p.dma("sync", dfp[buf][:, :],
                           wIn["wdn"][fg, :, 2 * k:2 * k + 2, :],
                           wait=[("po", dfp_free[buf])])
                cev = p.op("gpsimd",
                           lambda q, s=dfp[buf][:, :],
                           d=dbf[buf][:, :]: q.tensor_copy(d, s),
                           "po", wait=[("sLd", ld), ("pe", dbf_free[buf])])
                dfp_free[buf] = cev

                def dmm(q, k=k, buf=buf, banks=banks):
                    ins = None
                    for kcl in range(2):
                        for ftl in range(4):
                            ins = q.matmul(
                                banks[ftl][:, :],
                                dbf[buf][:, kcl * 512 + ftl * 128:
                                         kcl * 512 + ftl * 128 + 128],
                                qo[:, (2 * k + kcl) * 512:
                                   (2 * k + kcl) * 512 + 512],
                                start=(k == 0 and kcl == 0),
                                stop=(k == 31 and kcl == 1))
                    return ins
                w = [("po", cev)] + (mmw_first if k == 0 else [])
                dbf_free[buf] = p.op("tensor", dmm, "pe", wait=w)
            dn_ev = p.c["pe"]
            for ftl in range(4):
                ft = fg * 4 + ftl
                p.op("vector",
                     lambda q, d=t0[:, ft * 512:ft * 512 + 512],
                     s=banks[ftl], r=rst[:, ftl * 512:ftl * 512 + 512]:
                     q.tensor_add(d, s[:, :], r),
                     "ve", wait=[("pe", dn_ev), ("sLd", rld)])
                p.op("scalar",
                     lambda q, ft=ft,
                     d=sq4[:, (ft % 4) * 512:(ft % 4) * 512 + 512],
                     s=t0[:, ft * 512:ft * 512 + 512]:
                     q.activation(d, s, AF.Square),
                     "ac", wait=[("ve", p.c["ve"]), ("pe", prev_sev)])
            rst_ld = p.c["ve"]
            ev_ve, ev_ac = p.c["ve"], p.c["ac"]

            def pstat(q, fg=fg, banks=banks):
                ins = None
                for ftl in range(4):
                    ft = fg * 4 + ftl
                    q.matmul(banks[0][:, :], ones_b[:, :],
                             t0[:, ft * 512:ft * 512 + 512],
                             start=(ftl == 0), stop=(ftl == 3))
                    ins = q.matmul(banks[1][:, :], ones_b[:, :],
                                   sq4[:, (ft % 4) * 512:(ft % 4) * 512 + 512],
                                   start=(ftl == 0), stop=(ftl == 3))
                return ins
            sev = p.op("tensor", pstat, "pe",
                       wait=[("ve", ev_ve), ("ac", ev_ac)])
            prev_sev = sev
            if fg == 0:
                p.op("vector", lambda q, b=banks[0]:
                     q.tensor_copy(s1_sb[:, :], b[:, :]), "ve",
                     wait=[("pe", sev)])
                sf = p.op("vector", lambda q, b=banks[1]:
                          q.tensor_copy(s2_sb[:, :], b[:, :]), "ve")
            else:
                p.op("vector", lambda q, b=banks[0]:
                     q.tensor_add(s1_sb[:, :], s1_sb[:, :], b[:, :]), "ve",
                     wait=[("pe", sev)])
                sf = p.op("vector", lambda q, b=banks[1]:
                          q.tensor_add(s2_sb[:, :], s2_sb[:, :], b[:, :]),
                          "ve")
            set_free[s_] = sf
            stat_done = sf

        if th == 1 and "t0" in tap_t:
            p.dma("scalar", tap_t["t0"].rearrange("(a p) t -> p a t", p=128),
                  t0[:, :].rearrange("p (a t) -> p a t", t=512),
                  wait=[("ve", stat_done)], sem="sSt")
        e0 = p.op("vector",
                  lambda q: q.tensor_scalar(m_sb[:, :], s1_sb[:, :],
                                            1.0 / 4096.0, None, ALU.mult),
                  "ve", wait=[("ve", stat_done)])
        e1 = p.op("vector",
                  lambda q: q.tensor_scalar(v_sb[:, :], s2_sb[:, :],
                                            1.0 / 4096.0, None, ALU.mult),
                  "ve")
        e2 = p.op("scalar",
                  lambda q: q.activation(ln_t[:, :], m_sb[:, :], AF.Square),
                  "ac", wait=[("ve", e1)])
        e3 = p.op("vector",
                  lambda q: q.tensor_sub(v_sb[:, :], v_sb[:, :], ln_t[:, :]),
                  "ve", wait=[("ac", e2)])
        p.op("scalar",
             lambda q: q.activation(ln_t[:, :], v_sb[:, :], AF.Ln,
                                    bias=ceps[:, 0:1]),
             "ac", wait=[("ve", e3)])
        fin2 = p.op("scalar",
                    lambda q: q.activation(r_sb[:, :], ln_t[:, :], AF.Exp,
                                           scale=-0.5), "ac")

        for ft in range(32):
            sl = stream[:, ft * T_ + th * 512:ft * T_ + th * 512 + 512]
            t0s = t0[:, ft * 512:ft * 512 + 512]
            ve1 = p.op("vector",
                       lambda q, t0s=t0s:
                       q.tensor_sub(tmp1[:, :], t0s, m_sb[:, :]),
                       "ve", wait=[("ac", fin2)] if ft == 0 else None)
            ve2 = p.op("vector",
                       lambda q: q.tensor_mul(tmp2[:, :], tmp1[:, :],
                                              r_sb[:, :]), "ve")
            p.op("vector",
                 lambda q, ft=ft:
                 q.tensor_scalar(tmp1[:, :], tmp2[:, :],
                                 vS["gam"][:, ft:ft + 1],
                                 vS["bet"][:, ft:ft + 1],
                                 ALU.mult, ALU.add), "ve")
            ve3 = p.op("vector",
                       lambda q, sl=sl: q.tensor_add(sl, sl, tmp1[:, :]),
                       "ve")
            ac2 = p.op("scalar",
                       lambda q, sl=sl, ft=ft,
                       d=sq4[:, (ft % 4) * 512:(ft % 4) * 512 + 512]:
                       q.activation(d, sl, AF.Square),
                       "ac", wait=[("ve", ve3), ("pe", p.c["pe"])])

            def l3mm(q, ft=ft, sl=sl):
                q.matmul(ps[0][:, :], ones_b[:, :], sl,
                         start=(ft == 0), stop=(ft == 31))
                return q.matmul(ps[1][:, :], ones_b[:, :],
                                sq4[:, (ft % 4) * 512:(ft % 4) * 512 + 512],
                                start=(ft == 0), stop=(ft == 31))
            p.op("tensor", l3mm, "pe", wait=[("ve", ve3), ("ac", ac2)])
        l3_pe = p.c["pe"]
        fin3 = _ln_finalize(p, ps[0], ps[1], l3_pe, m_sb, r_sb, v_sb, ln_t,
                            1.0 / 4096.0, ceps)
        ost_free = 0
        for ft in range(32):
            sl = stream[:, ft * T_ + th * 512:ft * T_ + th * 512 + 512]
            oo = (ft % 2) * 512
            w = [("ac", fin3)] if ft == 0 else []
            if ft % 2 == 0:
                w.append(("sSt", ost_free))
            p.op("vector",
                 lambda q, sl=sl, oo=oo:
                 q.tensor_sub(ost[:, oo:oo + 512], sl, m_sb[:, :]),
                 "ve", wait=w or None)
            p.op("vector",
                 lambda q, oo=oo:
                 q.tensor_mul(ost[:, oo:oo + 512], ost[:, oo:oo + 512],
                              r_sb[:, :]), "ve")
            if ft % 2 == 1:
                ost_free = p.dma(
                    "scalar",
                    outT[ft - 1:ft + 1, :,
                         th * 512:th * 512 + 512].transpose([1, 0, 2]),
                    ost[:, :], wait=[("ve", p.c["ve"])], sem="sSt")
        out_st_total = p.c["sSt"]
    if "r2" in tap_t:
        p.dma("scalar", tap_t["r2"].rearrange("(a p) t -> p a t", p=128),
              stream[:, :].rearrange("p (a t) -> p a t", t=T),
              wait=[("ve", p.c["ve"])], sem="sSt")
    if "sb" in tap_t:
        p.dma("scalar", tap_t["sb"][:, :], s_b[:, :],
              wait=[("ve", p.c["ve"])], sem="sSt")
    if "lg" in tap_t:
        p.dma("scalar", tap_t["lg"][:, :], lg[:, :],
              wait=[("ve", p.c["ve"])], sem="sSt")
    if "st" in tap_t:
        p.dma("scalar", tap_t["st"][:, :], sT[:, :],
              wait=[("ve", p.c["ve"])], sem="sSt")


def mmgroup_n(p, wbf, buf, psA, psB, src, nk, mmwait):
    def gg(q, buf=buf, nk=nk):
        ins = None
        for k in range(nk):
            lh = wbf[buf][:, k * 128:k * 128 + 128]
            q.matmul(psA[:, :], lh, src[:, k * T:k * T + 512],
                     start=(k == 0), stop=(k == nk - 1))
            ins = q.matmul(psB[:, :], lh, src[:, k * T + 512:k * T + T],
                           start=(k == 0), stop=(k == nk - 1))
        return ins
    ev = p.op("tensor", gg, "pe", wait=mmwait)
    p.wbf_free[buf] = ev
    return ev


def kernel(**inputs):
    shared, percore = _stage(inputs)
    nc = build_program()
    in_maps = [dict(shared, **pc) for pc in percore]
    res = run_bass_kernel_spmd(nc, in_maps, core_ids=list(range(8)))
    out = np.empty((2, 4096, 4096), dtype=np.float32)
    for c in range(8):
        b, jg = c // 4, c % 4
        o = np.asarray(res.results[c]["outT"]).reshape(4096, T)
        out[b, jg * T:(jg + 1) * T, :] = o.T
    return out
